# version 03 (final)
# baseline (speedup 1.0000x reference)
import numpy as np
import ml_dtypes
import dataclasses
from contextlib import ExitStack
import concourse.bass as bass
import concourse.mybir as mybir

F32 = mybir.dt.float32
BF16 = mybir.dt.bfloat16
I32 = mybir.dt.int32
AF = mybir.ActivationFunctionType
ALU = mybir.AluOpType
AX = mybir.AxisListType

B, S, D = 2, 4096, 2048
DFF, NE = 1408, 6
HQ, NKV, DH, DKV = 16, 4, 128, 512
STRIP, ROWS = 512, 1024
NT = ROWS // 128
EXPM50 = float(np.exp(np.float64(-50.0)))
RT128 = float(np.sqrt(np.float64(128.0)))
FFN_STOP = 9


class Env:

    def __init__(self, name):
        self.nc = bass.Bass("TRN2", target_bir_lowering=False, debug=False)
        self.es = ExitStack()
        self.C = {}
        self.SEM = {}
        self.name = name

    def dram(self, name, shape, dt, kind):
        return self.nc.dram_tensor(name, shape, dt, kind=kind).ap()

    def sbuf(self, name, shape, dt):
        return self.es.enter_context(self.nc.sbuf_tensor(name, shape, dt))

    def psum(self, name, shape):
        return self.es.enter_context(self.nc.psum_tensor(name, shape, F32))

    def sem(self, name):
        s = self.es.enter_context(self.nc.semaphore(name))
        self.SEM[name] = s
        self.C[name] = 0
        return s

    def inc(self, ins, name, n):
        ins.then_inc(self.SEM[name], n)
        self.C[name] += n
        return self.C[name]


def cm(src, r0, r1, c0, c1):
    base = src[r0:r1, c0:c1]
    rstride = src.ap[0][0]
    nch = (r1 - r0) // 128
    return dataclasses.replace(
        base, ap=[[rstride, 128], [128 * rstride, nch], [1, c1 - c0]])


def pbc(dram_row_ap, n=128):
    ap = dram_row_ap
    new_ap = [[0, n]] + list(ap.ap[1:])
    return dataclasses.replace(ap, ap=new_ap)


def sap(base_ap, pattern):
    return dataclasses.replace(base_ap, ap=pattern)


def emit_ln(E, block, XR, BE, BD, ST_, ONE1, PS, target, with_gate=False,
            ent=("ld", "vs")):
    inc = E.inc
    C = E.C
    ld, vs, ss = E.SEM["ld"], E.SEM["vs"], E.SEM["ss"]

    def Xt(t):
        return XR[:, t * D:(t + 1) * D]

    sq_done, red_done, n_done, dma_done = [], [], [], []
    ENT_LD, ENT_VS = C["ld"], C["vs"]
    for t in range(NT):
        sc = 8 * t

        @block.scalar
        def _(s, t=t):
            if t == 0:
                s.wait_ge(ld, ENT_LD)
                s.wait_ge(vs, ENT_VS)
            if t > 0:
                s.wait_ge(vs, red_done[t - 1])
            ins = s.activation(out=BE[:, 0:2048], in_=Xt(t), func=AF.Square)
            sq_done.append(inc(ins, "ss", 1))

        @block.vector
        def _(v, t=t, sc=sc):
            v.wait_ge(ss, sq_done[t])
            v.reduce_sum(ST_[:, sc + 1:sc + 2], BE[:, 0:2048], axis=AX.X)
            ins = v.reduce_sum(ST_[:, sc + 0:sc + 1], Xt(t), axis=AX.X)
            red_done.append(inc(ins, "vs", 1))

    @block.vector
    def _(v):
        for t in range(NT):
            sc = 8 * t
            v.tensor_scalar_mul(ST_[:, sc + 2:sc + 3], ST_[:, sc + 0:sc + 1], 1.0 / D)
        for t in range(NT):
            sc = 8 * t
            v.tensor_tensor(ST_[:, sc + 3:sc + 4], ST_[:, sc + 2:sc + 3],
                            ST_[:, sc + 2:sc + 3], op=ALU.mult)
        for t in range(NT):
            sc = 8 * t
            v.scalar_tensor_tensor(out=ST_[:, sc + 4:sc + 5], in0=ST_[:, sc + 1:sc + 2],
                                   scalar=1.0 / D, in1=ST_[:, sc + 3:sc + 4],
                                   op0=ALU.mult, op1=ALU.subtract)
        for t in range(NT):
            sc = 8 * t
            v.tensor_scalar_add(ST_[:, sc + 5:sc + 6], ST_[:, sc + 4:sc + 5], 1e-5)
        for t in range(NT):
            sc = 8 * t
            ins = v.reciprocal(ST_[:, sc + 6:sc + 7], ST_[:, sc + 5:sc + 6])
        inc(ins, "vs", 1)
    RECIP = C["vs"]

    @block.scalar
    def _(s):
        s.wait_ge(vs, RECIP)
        for t in range(NT):
            sc = 8 * t
            ins = s.activation(out=ST_[:, sc + 7:sc + 8], in_=ST_[:, sc + 6:sc + 7],
                               func=AF.Sqrt)
        inc(ins, "ss", 1)
    SQRT = C["ss"]

    @block.vector
    def _(v):
        v.wait_ge(ss, SQRT)
        for t in range(NT):
            sc = 8 * t
            v.scalar_tensor_tensor(out=ST_[:, sc + 3:sc + 4], in0=ST_[:, sc + 2:sc + 3],
                                   scalar=-1.0, in1=ST_[:, sc + 7:sc + 8],
                                   op0=ALU.mult, op1=ALU.mult)
        ins = v.tensor_copy(BE[:, 0:2], ST_[:, 0:2])
        inc(ins, "vs", 1)
    NEGMR = C["vs"]

    for t in range(NT):
        sc = 8 * t

        @block.vector
        def _(v, t=t, sc=sc):
            if t == 0:
                v.wait_ge(vs, NEGMR)
            if t >= 2:
                v.wait_ge(ld, dma_done[t - 2])
            ins = v.tensor_scalar(out=BD[:, (t % 2) * D:(t % 2) * D + D], in0=Xt(t),
                                  scalar1=ST_[:, sc + 7:sc + 8],
                                  scalar2=ST_[:, sc + 3:sc + 4],
                                  op0=ALU.mult, op1=ALU.add)
            n_done.append(inc(ins, "vs", 1))

        @block.sync
        def _(sync, t=t):
            sync.wait_ge(vs, n_done[t])
            ins = sync.dma_start(out=target[t * 128:(t + 1) * 128, :],
                                 in_=BD[:, (t % 2) * D:(t % 2) * D + D])
            dma_done.append(inc(ins, "ld", 16))

        if with_gate:
            @block.tensor
            def _(tn, t=t, sc=sc):
                tn.wait_ge(vs, n_done[t])
                for ch in range(4):
                    ins = tn.matmul(PS[4 + ch][0:1, :], ST_[:, sc + 7:sc + 8],
                                    Xt(t)[:, ch * 512:(ch + 1) * 512],
                                    start=(t == 0), stop=(t == NT - 1))
                ins2 = tn.matmul(PS[3][0:1, 0:1], ST_[:, sc + 3:sc + 4], ONE1[:, :],
                                 start=(t == 0), stop=(t == NT - 1))
                if t == NT - 1:
                    inc(ins2, "ts", 1)
    out = {"n_done": n_done, "dma_done": dma_done}
    if with_gate:
        out["gate_ts"] = C["ts"]
    return out


def build_s0():
    E = Env("s0")
    x_own = E.dram("x_own", [ROWS, D], F32, "ExternalInput")
    o_n = E.dram("o_n", [ROWS, D], BF16, "ExternalOutput")
    with E.es:
        XR = E.sbuf("XR", [128, NT * D], F32)
        BE = E.sbuf("BE", [128, 2048], F32)
        BD = E.sbuf("BD", [128, 2 * D], BF16)
        ST_ = E.sbuf("ST_", [128, 64], F32)
        ONE1 = E.sbuf("ONE1", [128, 1], F32)
        PS = [E.psum(f"ps{i}", [128, 512]) for i in range(8)]
        for nm in ["ld", "vs", "ss", "ts", "od"]:
            E.sem(nm)
        ld, vs, ss, od = E.SEM["ld"], E.SEM["vs"], E.SEM["ss"], E.SEM["od"]
        block = E.es.enter_context(E.nc.Block())

        @block.sync
        def _(sync):
            E.inc(sync.dma_start(out=XR[:, :], in_=cm(x_own, 0, ROWS, 0, D)), "ld", 16)

        ln = emit_ln(E, block, XR, BE, BD, ST_, ONE1, PS, o_n, with_gate=False)

        @block.sync
        def _(sync):
            sync.wait_ge(ld, ln["dma_done"][-1])
    return E.nc


def build_s1():
    E = Env("s1")
    n_full = E.dram("n_full", [4096, D], BF16, "ExternalInput")
    wqa = E.dram("wqa", [D, 512], F32, "ExternalInput")
    wka = E.dram("wka", [D, D], F32, "ExternalInput")
    a_own = E.dram("a_own", [512, D], BF16, "ExternalOutput")
    q_dbg = E.dram("q_dbg", [128, 16384], BF16, "ExternalOutput")
    pt_dbg = E.dram("pt_dbg", [128, 8192], BF16, "ExternalOutput")
    with E.es:
        BA = E.sbuf("BA", [128, 16384], BF16)
        BB = E.sbuf("BB", [128, 16384], BF16)
        BC = E.sbuf("BC", [128, 8192], BF16)
        BD = E.sbuf("BD", [128, 8192], BF16)
        WS = E.sbuf("WS", [128, 8192], BF16)
        ST_ = E.sbuf("ST_", [128, 64], F32)
        PS = [E.psum(f"ps{i}", [128, 512]) for i in range(8)]
        for nm in ["ld", "gd", "ts", "vs", "ss", "tq", "od"]:
            E.sem(nm)
        ld, gd, ts, vs, ss, tq = (E.SEM[k] for k in ["ld", "gd", "ts", "vs", "ss", "tq"])
        inc, C = E.inc, E.C
        block = E.es.enter_context(E.nc.Block())

        @block.gpsimd
        def _(g):
            inc(g.dma_start(out=WS[:, 0:8192], in_=cm(wqa, 0, D, 0, 512)), "gd", 16)
        WQ_GD = C["gd"]

        q_slab, q_mm, q_ev = [], [], []
        for blk in range(4):
            @block.sync
            def _(sync, blk=blk):
                if blk >= 1:
                    sync.wait_ge(ts, q_mm[blk - 1])
                for dch in range(16):
                    ins = sync.dma_start(
                        out=BB[:, dch * 1024:(dch + 1) * 1024],
                        in_=n_full[blk * 1024:(blk + 1) * 1024, dch * 128:(dch + 1) * 128],
                        transpose=True)
                    inc(ins, "tq", 16)
                q_slab.append(C["tq"])

            @block.tensor
            def _(tn, blk=blk):
                if blk == 0:
                    tn.wait_ge(gd, WQ_GD)
                tn.wait_ge(tq, q_slab[blk])
                if blk >= 1:
                    tn.wait_ge(vs, q_ev[blk - 1])
                for dch in range(16):
                    for st in range(8):
                        ins = tn.matmul(PS[st][:, :],
                                        BB[:, dch * 1024 + st * 128:dch * 1024 + (st + 1) * 128],
                                        WS[:, dch * 512:(dch + 1) * 512],
                                        start=(dch == 0), stop=(dch == 15))
                q_mm.append(inc(ins, "ts", 1))

            @block.vector
            def _(v, blk=blk):
                v.wait_ge(ts, q_mm[blk])
                for st in range(8):
                    ins = v.tensor_copy(BA[:, (blk * 8 + st) * 512:(blk * 8 + st + 1) * 512],
                                        PS[st][:, :])
                q_ev.append(inc(ins, "vs", 1))

        p_tile, p_mm, p_ev = [], [], []
        for ph in range(2):
            for sg in range(32):
                @block.sync
                def _(sync, ph=ph, sg=sg):
                    i = ph * 32 + sg
                    if i >= 2:
                        sync.wait_ge(ts, p_mm[i - 2])
                    ins = sync.dma_start(out=BD[:, (sg % 2) * D:(sg % 2) * D + D],
                                         in_=n_full[sg * 128:(sg + 1) * 128, :])
                    p_tile.append(inc(ins, "ld", 16))

                @block.tensor
                def _(tn, ph=ph, sg=sg):
                    i = ph * 32 + sg
                    tn.wait_ge(ld, p_tile[i])
                    if sg == 0:
                        tn.wait_ge(vs, C["vs"])
                    for fc in range(8):
                        f = ph * 8 + fc
                        ins = tn.matmul(PS[fc][:, :],
                                        BD[:, (sg % 2) * D + f * 128:(sg % 2) * D + (f + 1) * 128],
                                        BA[:, sg * 512:(sg + 1) * 512],
                                        start=(sg == 0), stop=(sg == 31))
                    p_mm.append(inc(ins, "ts", 1))

            @block.vector
            def _(v, ph=ph):
                v.wait_ge(ts, p_mm[-1])
                for fc in range(8):
                    f = ph * 8 + fc
                    ins = v.tensor_copy(BC[:, f * 512:(f + 1) * 512], PS[fc][:, :])
                p_ev.append(inc(ins, "vs", 1))

        @block.gpsimd
        def _(g):
            g.wait_ge(vs, q_ev[-1])
            inc(g.dma_start(out=q_dbg[:, :], in_=BA[:, :]), "od", 16)
            g.wait_ge(vs, p_ev[-1])
            inc(g.dma_start(out=pt_dbg[:, :], in_=BC[:, :]), "od", 16)

        s_wl, s_mm, a_done = [], [], []
        for pr in range(2):
            for q4 in range(4):
                @block.gpsimd
                def _(g, pr=pr, q4=q4):
                    if (pr, q4) != (0, 0):
                        g.wait_ge(ts, s_mm[-1])
                    else:
                        g.wait_ge(ts, p_mm[-1])
                    ins = g.dma_start(out=WS[:, 0:8192],
                                      in_=cm(wka, 0, D, q4 * 512, (q4 + 1) * 512))
                    s_wl.append(inc(ins, "gd", 16))

                @block.tensor
                def _(tn, pr=pr, q4=q4):
                    tn.wait_ge(gd, s_wl[-1])
                    if (pr, q4) == (0, 0):
                        tn.wait_ge(vs, p_ev[-1])
                    if q4 == 0 and pr == 1:
                        tn.wait_ge(ss, C["ss"])
                    for e2 in range(2):
                        es_g = pr * 2 + e2
                        bank = e2 * 4 + q4
                        for fc in range(16):
                            ins = tn.matmul(
                                PS[bank][:, :],
                                BC[:, fc * 512 + es_g * 128:fc * 512 + (es_g + 1) * 128],
                                WS[:, fc * 512:(fc + 1) * 512],
                                start=(fc == 0), stop=(fc == 15))
                    s_mm.append(inc(ins, "ts", 1))

            sm_stat, sm_exp = [], []

            @block.vector
            def _(v, pr=pr):
                v.wait_ge(ts, s_mm[-1])
                for e2 in range(2):
                    sc = 32 + e2 * 8
                    for q4 in range(4):
                        v.reduce_max(ST_[:, sc + q4:sc + q4 + 1], PS[e2 * 4 + q4][:, :],
                                     axis=AX.X)
                    v.tensor_tensor(ST_[:, sc + 4:sc + 5], ST_[:, sc + 0:sc + 1],
                                    ST_[:, sc + 1:sc + 2], op=ALU.max)
                    v.tensor_tensor(ST_[:, sc + 5:sc + 6], ST_[:, sc + 2:sc + 3],
                                    ST_[:, sc + 3:sc + 4], op=ALU.max)
                    v.tensor_copy(ST_[:, sc + 7:sc + 8], ST_[:, sc + 4:sc + 5])
                    v.tensor_tensor(ST_[:, sc + 6:sc + 7], ST_[:, sc + 4:sc + 5],
                                    ST_[:, sc + 5:sc + 6], op=ALU.max)
                    v.tensor_copy(ST_[:, sc + 7:sc + 8], ST_[:, sc + 6:sc + 7])
                    ins = v.tensor_scalar_mul(ST_[:, sc + 7:sc + 8], ST_[:, sc + 6:sc + 7],
                                              -1.0 / 8.0)
                    sm_stat.append(inc(ins, "vs", 1))

            @block.scalar
            def _(s, pr=pr):
                for e2 in range(2):
                    sc = 32 + e2 * 8
                    s.wait_ge(vs, sm_stat[e2])
                    es_g = pr * 2 + e2
                    for q4 in range(4):
                        ins = s.activation(
                            out=BD[:, (es_g % 2) * 2048 + q4 * 512 + 4096:(es_g % 2) * 2048 + (q4 + 1) * 512 + 4096],
                            in_=PS[e2 * 4 + q4][:, :], func=AF.Exp,
                            bias=ST_[:, sc + 7:sc + 8], scale=0.125)
                    sm_exp.append(inc(ins, "ss", 1))

            @block.vector
            def _(v, pr=pr):
                for e2 in range(2):
                    sc = 32 + e2 * 8
                    es_g = pr * 2 + e2
                    v.wait_ge(ss, sm_exp[e2])
                    v.reduce_sum(ST_[:, sc + 0:sc + 1],
                                 BD[:, (es_g % 2) * 2048 + 4096:(es_g % 2) * 2048 + 6144],
                                 axis=AX.X)
                    v.tensor_copy(ST_[:, sc + 2:sc + 3], ST_[:, sc + 0:sc + 1])
                    v.reciprocal(ST_[:, sc + 1:sc + 2], ST_[:, sc + 0:sc + 1])
                    v.tensor_copy(ST_[:, sc + 3:sc + 4], ST_[:, sc + 1:sc + 2])
                    ins = v.tensor_scalar_mul(
                        BD[:, (es_g % 2) * 2048:(es_g % 2) * 2048 + 2048],
                        BD[:, (es_g % 2) * 2048 + 4096:(es_g % 2) * 2048 + 6144],
                        ST_[:, sc + 1:sc + 2])
                    a_done.append(inc(ins, "vs", 1))

            @block.sync
            def _(sync, pr=pr):
                for e2 in range(2):
                    es_g = pr * 2 + e2
                    sync.wait_ge(vs, a_done[es_g])
                    ins = sync.dma_start(out=a_own[es_g * 128:(es_g + 1) * 128, :],
                                        in_=BD[:, (es_g % 2) * 2048:(es_g % 2) * 2048 + 2048])
                    inc(ins, "od", 16)

        @block.sync
        def _(sync):
            sync.wait_ge(E.SEM["od"], C["od"])
    return E.nc


def build_s2():
    E = Env("s2")
    a_full = E.dram("a_full", [D, D], BF16, "ExternalInput")
    woa = E.dram("woa", [D, 512], F32, "ExternalInput")
    m_own = E.dram("m_own", [D, 512], BF16, "ExternalOutput")
    with E.es:
        BD = E.sbuf("BD", [128, 8192], BF16)
        BC = E.sbuf("BC", [128, 8192], BF16)
        WS = E.sbuf("WS", [128, 8192], BF16)
        PS = [E.psum(f"ps{i}", [128, 512]) for i in range(8)]
        for nm in ["ld", "gd", "ts", "vs", "od"]:
            E.sem(nm)
        ld, gd, ts, vs, od = (E.SEM[k] for k in ["ld", "gd", "ts", "vs", "od"])
        inc, C = E.inc, E.C
        block = E.es.enter_context(E.nc.Block())

        @block.gpsimd
        def _(g):
            inc(g.dma_start(out=WS[:, 0:8192], in_=cm(woa, 0, D, 0, 512)), "gd", 16)
        WO_GD = C["gd"]

        m_tile, m_mm, m_ev = [], [], []
        for ph in range(2):
            for ec in range(16):
                @block.sync
                def _(sync, ph=ph, ec=ec):
                    i = ph * 16 + ec
                    if i >= 2:
                        sync.wait_ge(ts, m_mm[i - 2])
                    ins = sync.dma_start(out=BD[:, (ec % 2) * 1024:(ec % 2) * 1024 + 1024],
                                         in_=a_full[ec * 128:(ec + 1) * 128,
                                                    ph * 1024:(ph + 1) * 1024])
                    m_tile.append(inc(ins, "ld", 16))

                @block.tensor
                def _(tn, ph=ph, ec=ec):
                    i = ph * 16 + ec
                    tn.wait_ge(ld, m_tile[i])
                    tn.wait_ge(gd, WO_GD)
                    if ec == 0:
                        tn.wait_ge(vs, C["vs"])
                    for ft in range(8):
                        ins = tn.matmul(PS[ft][:, :],
                                        BD[:, (ec % 2) * 1024 + ft * 128:(ec % 2) * 1024 + (ft + 1) * 128],
                                        WS[:, ec * 512:(ec + 1) * 512],
                                        start=(ec == 0), stop=(ec == 15))
                    m_mm.append(inc(ins, "ts", 1))

            @block.vector
            def _(v, ph=ph):
                v.wait_ge(ts, m_mm[-1])
                for ft in range(8):
                    f = ph * 8 + ft
                    ins = v.tensor_copy(BC[:, f * 512:(f + 1) * 512], PS[ft][:, :])
                m_ev.append(inc(ins, "vs", 1))

        @block.sync
        def _(sync):
            sync.wait_ge(vs, m_ev[-1])
            ins = sync.dma_start(out=cm(m_own, 0, D, 0, 512), in_=BC[:, :])
            inc(ins, "od", 16)
            sync.wait_ge(od, C["od"])
    return E.nc


def build_s3():
    E = Env("s3")
    x_own = E.dram("x_own", [ROWS, D], F32, "ExternalInput")
    n_own = E.dram("n_own", [ROWS, D], BF16, "ExternalInput")
    m_full = E.dram("m_full", [4 * D, 512], BF16, "ExternalInput")
    wva = E.dram("wva", [D, D], F32, "ExternalInput")
    x1_own = E.dram("x1_own", [ROWS, D], F32, "ExternalOutput")
    n2_own = E.dram("n2_own", [ROWS, D], BF16, "ExternalOutput")
    g1p = E.dram("g1p", [1, D], F32, "ExternalOutput")
    with E.es:
        XR = E.sbuf("XR", [128, NT * D], F32)
        BA = E.sbuf("BA", [128, 16384], BF16)
        BB = E.sbuf("BB", [128, 16384], BF16)
        BD = E.sbuf("BD", [128, 8192], BF16)
        BE = E.sbuf("BE", [128, 2048], F32)
        WS = E.sbuf("WS", [128, 8192], BF16)
        ST_ = E.sbuf("ST_", [128, 64], F32)
        ONE1 = E.sbuf("ONE1", [128, 1], F32)
        PS = [E.psum(f"ps{i}", [128, 512]) for i in range(8)]
        for nm in ["ld", "gd", "ts", "vs", "ss", "tq", "od"]:
            E.sem(nm)
        ld, gd, ts, vs, ss, tq, od = (E.SEM[k] for k in
                                      ["ld", "gd", "ts", "vs", "ss", "tq", "od"])
        inc, C = E.inc, E.C
        block = E.es.enter_context(E.nc.Block())

        @block.sync
        def _(sync):
            inc(sync.dma_start(out=XR[:, :], in_=cm(x_own, 0, ROWS, 0, D)), "ld", 16)
            for s2 in range(2):
                for dch in range(16):
                    ins = sync.dma_start(
                        out=BB[:, (s2 * 16 + dch) * 512:(s2 * 16 + dch + 1) * 512],
                        in_=n_own[s2 * 512:(s2 + 1) * 512, dch * 128:(dch + 1) * 128],
                        transpose=True)
                    inc(ins, "tq", 16)
        XLD = C["ld"]
        NSLAB = C["tq"]

        @block.vector
        def _(v):
            inc(v.memset(ONE1[:, :], 1.0), "vs", 1)
        VS0 = C["vs"]

        v_wl, v_mm, v_ev = [], [], []
        for q4 in range(4):
            @block.gpsimd
            def _(g, q4=q4):
                if q4 >= 1:
                    g.wait_ge(ts, v_mm[-1])
                ins = g.dma_start(out=WS[:, 0:8192],
                                  in_=cm(wva, 0, D, q4 * 512, (q4 + 1) * 512))
                v_wl.append(inc(ins, "gd", 16))

            @block.tensor
            def _(tn, q4=q4):
                tn.wait_ge(gd, v_wl[q4])
                tn.wait_ge(tq, NSLAB)
                if q4 >= 1:
                    tn.wait_ge(vs, v_ev[-1])
                for s2 in range(2):
                    for ft4 in range(4):
                        bank = s2 * 4 + ft4
                        for dch in range(16):
                            ins = tn.matmul(
                                PS[bank][:, :],
                                WS[:, dch * 512 + ft4 * 128:dch * 512 + (ft4 + 1) * 128],
                                BB[:, (s2 * 16 + dch) * 512:(s2 * 16 + dch + 1) * 512],
                                start=(dch == 0), stop=(dch == 15))
                v_mm.append(inc(ins, "ts", 1))

            @block.vector
            def _(v, q4=q4):
                v.wait_ge(ts, v_mm[q4])
                for s2 in range(2):
                    for ft4 in range(4):
                        f = q4 * 4 + ft4
                        ins = v.tensor_copy(BA[:, f * 1024 + s2 * 512:f * 1024 + (s2 + 1) * 512],
                                            PS[s2 * 4 + ft4][:, :])
                v_ev.append(inc(ins, "vs", 1))

        y1_tile, y1_mm, y1_ev = [], [], []
        for gr in range(4):
            for fc in range(16):
                @block.sync
                def _(sync, gr=gr, fc=fc):
                    i = gr * 16 + fc
                    if i >= 2:
                        sync.wait_ge(ts, y1_mm[i - 2])
                    ins = sync.dma_start(out=BD[:, (fc % 2) * 512:(fc % 2) * 512 + 512],
                                         in_=m_full[gr * 2048 + fc * 128:gr * 2048 + (fc + 1) * 128, :])
                    y1_tile.append(inc(ins, "ld", 16))

                @block.tensor
                def _(tn, gr=gr, fc=fc):
                    i = gr * 16 + fc
                    tn.wait_ge(ld, y1_tile[i])
                    if fc == 0:
                        tn.wait_ge(vs, C["vs"])
                    for st in range(8):
                        ins = tn.matmul(PS[st][:, :],
                                        BA[:, fc * 1024 + st * 128:fc * 1024 + (st + 1) * 128],
                                        BD[:, (fc % 2) * 512:(fc % 2) * 512 + 512],
                                        start=(fc == 0), stop=(fc == 15))
                    y1_mm.append(inc(ins, "ts", 1))

            @block.vector
            def _(v, gr=gr):
                v.wait_ge(ts, y1_mm[-1])
                if gr == 0:
                    v.wait_ge(ld, XLD)
                for st in range(8):
                    ins = v.tensor_tensor(XR[:, st * D + gr * 512:st * D + (gr + 1) * 512],
                                          XR[:, st * D + gr * 512:st * D + (gr + 1) * 512],
                                          PS[st][:, :], op=ALU.add)
                y1_ev.append(inc(ins, "vs", 1))

        @block.sync
        def _(sync):
            sync.wait_ge(vs, y1_ev[-1])
            inc(sync.dma_start(out=cm(x1_own, 0, ROWS, 0, D), in_=XR[:, :]), "od", 16)

        ln = emit_ln(E, block, XR, BE, BD, ST_, ONE1, PS, n2_own, with_gate=True)

        @block.vector
        def _(v):
            v.wait_ge(ts, ln["gate_ts"])
            v.tensor_copy(ST_[0:1, 62:63], PS[3][0:1, 0:1])
            v.tensor_copy(ST_[0:1, 61:62], ST_[0:1, 62:63])
            for ch in range(4):
                ins = v.tensor_scalar(out=BE[0:1, ch * 512:(ch + 1) * 512],
                                      in0=PS[4 + ch][0:1, :], scalar1=ST_[0:1, 62:63],
                                      scalar2=None, op0=ALU.add)
            inc(ins, "vs", 1)
        GEV = C["vs"]

        @block.sync
        def _(sync):
            sync.wait_ge(vs, GEV)
            inc(sync.dma_start(out=g1p[0:1, :], in_=BE[0:1, 0:2048]), "od", 16)
            sync.wait_ge(ld, ln["dma_done"][-1])
            sync.wait_ge(od, C["od"])
    return E.nc


def emit_ffn(E, block, XR, BA, BB, BD, BE, WS, ST_, GS, G128, GWTS, MEANT,
             IOW_, IDXF, IDXI, IDXJ, IDXG, B1S, B2S, B2C, ONE1, PS,
             g_all, w1f, w2f, B1ALL, B2ALL, gbd, b2cd, w1e, w2e, arg_s,
             SELR, ID128):
    inc, C = E.inc, E.C
    ld, gd, ts, vs, ss = (E.SEM[k] for k in ["ld", "gd", "ts", "vs", "ss"])

    if FFN_STOP <= 0:
        return C["vs"]
    @block.sync
    def _(sync):
        ins = sync.dma_start(out=BE[0:64, 0:128], in_=g_all[:, :])
        inc(ins, "ld", 16)
    G4L = C["ld"]

    @block.tensor
    def _(tn):
        tn.wait_ge(ld, G4L)
        tn.wait_ge(vs, C["vs"])
        tn.wait_ge(ss, C["ss"])
        ins = tn.matmul(PS[0][:, 0:16], BE[0:64, 0:128], SELR[0:64, 0:16],
                        start=True, stop=True)
        inc(ins, "ts", 1)
    GSUM = C["ts"]

    @block.vector
    def _(v):
        v.wait_ge(ts, GSUM)
        ins = v.tensor_scalar_mul(MEANT[:, :], PS[0][:, 0:16], 1.0 / 4096.0)
        inc(ins, "vs", 1)
    MTS = C["vs"]
    if FFN_STOP == 11:
        return C["vs"]

    @block.tensor
    def _(tn):
        tn.wait_ge(vs, MTS)
        for c_ in range(16):
            ins = tn.matmul(PS[1][0:1, 0:6], MEANT[:, c_:c_ + 1],
                            GWTS[:, 6 * c_:6 * c_ + 6],
                            start=(c_ == 0), stop=(c_ == 15))
        inc(ins, "ts", 1)
    GMM = C["ts"]
    if FFN_STOP == 12:
        return C["vs"]

    @block.vector
    def _(v):
        v.wait_ge(ts, GMM)
        v.memset(GS[0:1, 0:8], -1e30)
        v.tensor_copy(GS[0:1, 60:62], IOW_[0:1, 0:2])
        v.tensor_copy(GS[0:1, 0:6], PS[1][0:1, 0:6])
        v.tensor_copy(GS[0:1, 60:62], IOW_[0:1, 0:2])
        v.reduce_max(GS[0:1, 8:9], GS[0:1, 0:8], axis=AX.X)
        v.tensor_copy(GS[0:1, 60:62], IOW_[0:1, 0:2])
        v.tensor_scalar(out=GS[0:1, 9:17], in0=GS[0:1, 0:8], scalar1=GS[0:1, 8:9],
                        scalar2=None, op0=ALU.is_equal)
        v.tensor_copy(GS[0:1, 60:62], IOW_[0:1, 0:2])
        v.scalar_tensor_tensor(out=GS[0:1, 17:25], in0=GS[0:1, 9:17], scalar=-2e30,
                               in1=GS[0:1, 0:8], op0=ALU.mult, op1=ALU.add)
        v.tensor_copy(GS[0:1, 60:62], IOW_[0:1, 0:2])
        v.reduce_max(GS[0:1, 25:26], GS[0:1, 17:25], axis=AX.X)
        v.tensor_copy(GS[0:1, 60:62], IOW_[0:1, 0:2])
        v.tensor_scalar(out=GS[0:1, 26:34], in0=GS[0:1, 0:8], scalar1=GS[0:1, 25:26],
                        scalar2=None, op0=ALU.is_equal)
        v.tensor_copy(GS[0:1, 60:62], IOW_[0:1, 0:2])
        v.tensor_tensor(GS[0:1, 34:35], GS[0:1, 8:9], GS[0:1, 25:26], op=ALU.subtract)
        v.tensor_tensor(GS[0:1, 40:48], GS[0:1, 9:17], IOW_[0:1, 0:8], op=ALU.mult)
        v.tensor_copy(GS[0:1, 60:62], IOW_[0:1, 0:2])
        v.reduce_sum(GS[0:1, 35:36], GS[0:1, 40:48], axis=AX.X)
        v.tensor_tensor(GS[0:1, 40:48], GS[0:1, 26:34], IOW_[0:1, 0:8], op=ALU.mult)
        v.tensor_copy(GS[0:1, 60:62], IOW_[0:1, 0:2])
        ins = v.reduce_sum(GS[0:1, 36:37], GS[0:1, 40:48], axis=AX.X)
        inc(ins, "vs", 1)
    T2A = C["vs"]
    if FFN_STOP == 13:
        return C["vs"]

    @block.scalar
    def _(s):
        s.wait_ge(vs, T2A)
        ins = s.activation(out=GS[0:1, 37:38], in_=GS[0:1, 34:35], func=AF.Sigmoid)
        inc(ins, "ss", 1)
    SGD = C["ss"]

    @block.vector
    def _(v):
        v.wait_ge(ss, SGD)
        v.tensor_copy(GS[0:1, 48:49], GS[0:1, 37:38])
        v.tensor_scalar(out=GS[0:1, 49:50], in0=GS[0:1, 37:38], scalar1=-1.0,
                        scalar2=1.0, op0=ALU.mult, op1=ALU.add)
        v.tensor_scalar_mul(GS[0:1, 50:51], GS[0:1, 35:36], 16.0)
        v.tensor_scalar_mul(GS[0:1, 51:52], GS[0:1, 36:37], 16.0)
        v.tensor_scalar_mul(GS[0:1, 52:53], GS[0:1, 35:36], 11.0)
        v.tensor_scalar_mul(GS[0:1, 53:54], GS[0:1, 36:37], 11.0)
        v.tensor_copy(GS[0:1, 54:55], GS[0:1, 35:36])
        v.tensor_copy(GS[0:1, 55:56], GS[0:1, 36:37])
        ins = v.tensor_copy(GS[0:1, 56:62], GS[0:1, 9:15])
        inc(ins, "vs", 1)
    GBV = C["vs"]

    @block.sync
    def _(sync):
        sync.wait_ge(vs, GBV)
        ins = sync.dma_start(out=gbd[0:1, :], in_=GS[0:1, 48:64])
        inc(ins, "ld", 16)
        ins = sync.dma_start(out=gbd[1:2, :], in_=GS[0:1, 24:40])
        inc(ins, "ld", 16)
        sync.wait_ge(ld, C["ld"])
        ins = sync.dma_start(out=G128[:, :], in_=sap(gbd[0:1, 0:16], [[0, 128], [16, 2], [1, 16]]))
        inc(ins, "ld", 16)
    G128L = C["ld"]
    if FFN_STOP == 14:
        return C["vs"]

    @block.vector
    def _(v):
        v.wait_ge(ld, G128L)
        for k in range(2):
            mb = 8 if k == 0 else 18
            v.tensor_scalar(out=B1S[:, k * 11:(k + 1) * 11], in0=B1ALL[:, 0:11],
                            scalar1=G128[:, mb:mb + 1], scalar2=None, op0=ALU.mult)
            v.tensor_scalar(out=B2S[:, k * 16:(k + 1) * 16], in0=B2ALL[:, 0:16],
                            scalar1=G128[:, mb:mb + 1], scalar2=None, op0=ALU.mult)
        for e_ in range(1, NE):
            for k in range(2):
                mb = 8 if k == 0 else 18
                v.scalar_tensor_tensor(out=B1S[:, k * 11:(k + 1) * 11],
                                       in0=B1ALL[:, e_ * 11:(e_ + 1) * 11],
                                       scalar=G128[:, mb + e_:mb + e_ + 1],
                                       in1=B1S[:, k * 11:(k + 1) * 11],
                                       op0=ALU.mult, op1=ALU.add)
                v.scalar_tensor_tensor(out=B2S[:, k * 16:(k + 1) * 16],
                                       in0=B2ALL[:, e_ * 16:(e_ + 1) * 16],
                                       scalar=G128[:, mb + e_:mb + e_ + 1],
                                       in1=B2S[:, k * 16:(k + 1) * 16],
                                       op0=ALU.mult, op1=ALU.add)
        ins = v.tensor_copy(GS[0:1, 60:62], IOW_[0:1, 0:2])
        inc(ins, "vs", 1)
    BGD = C["vs"]
    if FFN_STOP == 15:
        return C["vs"]

    @block.vector
    def _(v):
        v.wait_ge(vs, BGD)
        v.tensor_scalar(out=B2C[:, 0:16], in0=B2S[:, 0:16], scalar1=G128[:, 0:1],
                        scalar2=None, op0=ALU.mult)
        v.tensor_copy(GS[0:1, 60:62], IOW_[0:1, 0:2])
        ins = v.scalar_tensor_tensor(out=B2C[:, 0:16], in0=B2S[:, 16:32],
                                     scalar=G128[:, 1:2], in1=B2C[:, 0:16],
                                     op0=ALU.mult, op1=ALU.add)
        inc(ins, "vs", 1)
    B2CV = C["vs"]
    if FFN_STOP == 16:
        return C["vs"]

    @block.tensor
    def _(tn):
        tn.wait_ge(vs, B2CV)
        ins = tn.matmul(PS[1][0:16, 0:128], B2C[:, 0:16], ID128[:, :],
                        start=True, stop=True)
        inc(ins, "ts", 1)
    B2CT = C["ts"]

    @block.vector
    def _(v):
        v.wait_ge(ts, B2CT)
        ins = v.tensor_copy(BD[0:16, 0:128], PS[1][0:16, 0:128])
        inc(ins, "vs", 1)
    B2CE = C["vs"]

    @block.sync
    def _(sync):
        sync.wait_ge(vs, B2CE)
        ins = sync.dma_start(out=b2cd[:, :], in_=BD[0:16, 0:128])
        inc(ins, "ld", 16)
    B2CL = C["ld"]
    if FFN_STOP == 17:
        return C["vs"]

    if FFN_STOP <= 1:
        return C["vs"]
    prev_h_ld_last = None
    prev_y2_ld_last = None
    prev_h_mm_last = None
    for k in range(2):
        @block.vector
        def _(v, k=k):
            v.wait_ge(ld, G128L)
            if prev_y2_ld_last is not None:
                v.wait_ge(gd, C["gd"])
            v.tensor_scalar(out=IDXF[:, 0:16], in0=IOW_[:, 0:16],
                            scalar1=G128[:, 2 + k:3 + k], scalar2=None, op0=ALU.add)
            v.tensor_scalar(out=IDXG[:, 0:11], in0=IOW_[:, 0:11],
                            scalar1=G128[:, 4 + k:5 + k], scalar2=None, op0=ALU.add)
            v.tensor_copy(GS[0:1, 60:62], IOW_[0:1, 0:2])
            v.tensor_copy(GS[0:1, 60:62], IOW_[0:1, 0:2])
            v.tensor_copy(IDXI[:, 0:16], IDXF[:, 0:16])
            ins = v.tensor_copy(IDXJ[:, 0:11], IDXG[:, 0:11])
            inc(ins, "vs", 1)
        IDXV = C["vs"]

        g_w1, s_w1 = [], []
        for dch in range(16):
            @block.gpsimd
            def _(g, dch=dch, k=k):
                if dch == 0:
                    g.wait_ge(vs, IDXV)
                    if prev_h_ld_last is not None:
                        g.wait_ge(ld, prev_h_ld_last)
                if dch >= 2:
                    g.wait_ge(gd, s_w1[dch - 2])
                ins = g.indirect_dma_start(
                    out=BE[:, (dch % 2) * 1408:(dch % 2) * 1408 + 1408],
                    out_offset=None, in_=w1f[:, :],
                    in_offset=bass.IndirectOffsetOnAxis(ap=IDXI[:, dch:dch + 1], axis=0))
                g_w1.append(inc(ins, "gd", 16))
                g.wait_ge(gd, g_w1[-1])
                ins = g.dma_start(out=w1e[dch * 128:(dch + 1) * 128, :],
                                  in_=BE[:, (dch % 2) * 1408:(dch % 2) * 1408 + 1408])
                s_w1.append(inc(ins, "gd", 16))
        W1E = C["gd"]

        g_w2, s_w2 = [], []
        for fch in range(11):
            @block.gpsimd
            def _(g, fch=fch, k=k):
                if fch == 0:
                    g.wait_ge(gd, W1E)
                    if prev_y2_ld_last is not None:
                        g.wait_ge(ld, prev_y2_ld_last)
                ins = g.indirect_dma_start(
                    out=BE[:, 0:2048], out_offset=None, in_=w2f[:, :],
                    in_offset=bass.IndirectOffsetOnAxis(ap=IDXJ[:, fch:fch + 1], axis=0))
                g_w2.append(inc(ins, "gd", 16))
                g.wait_ge(gd, g_w2[-1])
                ins = g.dma_start(out=w2e[fch * 128:(fch + 1) * 128, :],
                                  in_=BE[:, 0:2048])
                s_w2.append(inc(ins, "gd", 16))
        W2E = C["gd"]

        if FFN_STOP <= 2:
            continue
        h_ld, h_mm, h_ev = [], [], []
        for s2 in range(2):
            for (f0, f1) in ((0, 8), (8, 11)):
                for dch in range(16):
                    @block.sync
                    def _(sync, s2=s2, f0=f0, f1=f1, dch=dch):
                        i = len(h_ld)
                        if i == 0:
                            sync.wait_ge(gd, W2E)
                        if i < 2 and prev_h_mm_last is not None:
                            sync.wait_ge(ts, prev_h_mm_last)
                        if i >= 2:
                            sync.wait_ge(ts, h_mm[i - 2])
                        w = (f1 - f0) * 128
                        ins = sync.dma_start(
                            out=WS[:, (dch % 2) * 1024:(dch % 2) * 1024 + w],
                            in_=w1e[dch * 128:(dch + 1) * 128, f0 * 128:f1 * 128])
                        h_ld.append(inc(ins, "ld", 16))

                    @block.tensor
                    def _(tn, s2=s2, f0=f0, f1=f1, dch=dch):
                        i = len(h_mm)
                        tn.wait_ge(ld, h_ld[i])
                        if dch == 0:
                            tn.wait_ge(ss, C["ss"])
                            tn.wait_ge(vs, C["vs"])
                        for ft in range(f0, f1):
                            ins = tn.matmul(
                                PS[ft - f0][:, :],
                                WS[:, (dch % 2) * 1024 + (ft - f0) * 128:(dch % 2) * 1024 + (ft - f0 + 1) * 128],
                                BB[:, (s2 * 16 + dch) * 512:(s2 * 16 + dch + 1) * 512],
                                start=(dch == 0), stop=(dch == 15))
                        h_mm.append(inc(ins, "ts", 1))

                @block.scalar
                def _(s, s2=s2, f0=f0, f1=f1, k=k):
                    s.wait_ge(ts, h_mm[-1])
                    s.wait_ge(vs, BGD)
                    for ft in range(f0, f1):
                        ins = s.activation(
                            out=BA[:, ft * 1024 + s2 * 512:ft * 1024 + (s2 + 1) * 512],
                            in_=PS[ft - f0][:, :], func=AF.Gelu,
                            bias=B1S[:, k * 11 + ft:k * 11 + ft + 1], scale=1.0)
                    h_ev.append(inc(ins, "ss", 1))
        HDONE = C["ss"]
        prev_h_ld_last = h_ld[-1]
        prev_h_mm_last = h_mm[-1]

        if FFN_STOP <= 3:
            continue
        y2_ld, y2_mm = [], []
        for s2 in range(2):
            for dqp in range(2):
                for fch in range(11):
                    @block.sync
                    def _(sync, s2=s2, dqp=dqp, fch=fch):
                        i = len(y2_ld)
                        if i == 0:
                            sync.wait_ge(gd, W2E)
                            sync.wait_ge(ss, HDONE)
                        if i >= 1 and i % 11 == 0:
                            sync.wait_ge(ts, y2_mm[i - 1])
                        elif i >= 2:
                            sync.wait_ge(ts, y2_mm[i - 2])
                        ins = sync.dma_start(
                            out=WS[:, 4096 + (fch % 2) * 1024:4096 + (fch % 2) * 1024 + 1024],
                            in_=w2e[fch * 128:(fch + 1) * 128, dqp * 1024:(dqp + 1) * 1024])
                        y2_ld.append(inc(ins, "ld", 16))

                    @block.tensor
                    def _(tn, s2=s2, dqp=dqp, fch=fch, k=k):
                        i = len(y2_mm)
                        tn.wait_ge(ld, y2_ld[i])
                        if fch == 0:
                            tn.wait_ge(vs, C["vs"])
                        for dq2 in range(2):
                            for st4 in range(4):
                                ins = tn.matmul(
                                    PS[dq2 * 4 + st4][:, :],
                                    BA[:, fch * 1024 + s2 * 512 + st4 * 128:fch * 1024 + s2 * 512 + (st4 + 1) * 128],
                                    WS[:, 4096 + (fch % 2) * 1024 + dq2 * 512:4096 + (fch % 2) * 1024 + (dq2 + 1) * 512],
                                    start=(fch == 0), stop=(fch == 10))
                        y2_mm.append(inc(ins, "ts", 1))

                @block.vector
                def _(v, s2=s2, dqp=dqp, k=k):
                    v.wait_ge(ts, y2_mm[-1])
                    for dq2 in range(2):
                        for st4 in range(4):
                            t = s2 * 4 + st4
                            dc = dqp * 2 + dq2
                            ins = v.scalar_tensor_tensor(
                                out=XR[:, t * D + dc * 512:t * D + (dc + 1) * 512],
                                in0=PS[dq2 * 4 + st4][:, :],
                                scalar=G128[:, k:k + 1],
                                in1=XR[:, t * D + dc * 512:t * D + (dc + 1) * 512],
                                op0=ALU.mult, op1=ALU.add)
                    inc(ins, "vs", 1)
        prev_y2_ld_last = y2_ld[-1]

    if FFN_STOP <= 4:
        return C["vs"]
    @block.gpsimd
    def _(g):
        g.wait_ge(ld, B2CL)
        g.wait_ge(vs, C["vs"])
        flat = dataclasses.replace(b2cd[0:1, 0:128], ap=[[0, 128], [1, 2048]])
        ins = g.dma_start(out=BE[:, 0:2048], in_=flat)
        inc(ins, "gd", 16)
    B2BB = C["gd"]

    @block.vector
    def _(v):
        v.wait_ge(gd, B2BB)
        for t in range(NT):
            ins = v.tensor_tensor(XR[:, t * D:(t + 1) * D], XR[:, t * D:(t + 1) * D],
                                  BE[:, 0:2048], op=ALU.add)
        inc(ins, "vs", 1)
    return C["vs"]


def ffn_stage_common(E, with_proj, final):
    x_in = E.dram("x_in", [ROWS, D], F32, "ExternalInput")
    n_in = E.dram("n_in", [ROWS, D], BF16, "ExternalInput")
    g_all = E.dram("g_all", [64, 128], F32, "ExternalInput")
    w1f = E.dram("w1f", [NE * D, DFF], F32, "ExternalInput")
    w2f = E.dram("w2f", [NE * DFF, D], F32, "ExternalInput")
    b1t_in = E.dram("b1t", [NE * 128, 11], F32, "ExternalInput")
    b2t_in = E.dram("b2t", [NE * 128, 16], F32, "ExternalInput")
    gwt = E.dram("gwt", [128, 16 * NE], F32, "ExternalInput")
    iow = E.dram("iow", [128, 16], F32, "ExternalInput")
    selr = E.dram("selr", [64, 16], F32, "ExternalInput")
    id128 = E.dram("id128", [128, 128], F32, "ExternalInput")
    if with_proj:
        wkb = E.dram("wkb", [D, DKV], F32, "ExternalInput")
        wvb = E.dram("wvb", [D, DKV], F32, "ExternalInput")
        kT_own = E.dram("kT_own", [DKV, ROWS], BF16, "ExternalOutput")
        v_own = E.dram("v_own", [ROWS, DKV], BF16, "ExternalOutput")
    x_out = E.dram("x_out", [ROWS, D], F32, "ExternalOutput")
    if not final:
        n_out = E.dram("n_out", [ROWS, D], BF16, "ExternalOutput")
        if not with_proj:
            g_out = E.dram("g_out", [1, D], F32, "ExternalOutput")
    gbd = E.dram("gbd", [2, 16], F32, "Internal")
    b2cd = E.dram("b2cd", [16, 128], BF16, "Internal")
    arg_s = E.dram("arg_s", [1, D], F32, "Internal")
    w1e = E.dram("w1e", [D, DFF], BF16, "ExternalOutput" if FFN_STOP == 2 else "Internal")
    w2e = E.dram("w2e", [DFF, D], BF16, "ExternalOutput" if FFN_STOP == 2 else "Internal")

    with E.es:
        XR = E.sbuf("XR", [128, NT * D], F32)
        BA = E.sbuf("BA", [128, 16384], BF16)
        BB = E.sbuf("BB", [128, 16384], BF16)
        BD = E.sbuf("BD", [128, 8192], BF16)
        BE = E.sbuf("BE", [128, 2816], F32)
        WS = E.sbuf("WS", [128, 8192], BF16)
        ST_ = E.sbuf("ST_", [128, 64], F32)
        GS = E.sbuf("GS", [1, 64], F32)
        G128 = E.sbuf("G128", [128, 32], F32)
        GWTS = E.sbuf("GWTS", [128, 16 * NE], F32)
        MEANT = E.sbuf("MEANT", [128, 16], F32)
        IOW_ = E.sbuf("IOW_", [128, 16], F32)
        IDXF = E.sbuf("IDXF", [128, 16], F32)
        IDXI = E.sbuf("IDXI", [128, 16], I32)
        IDXJ = E.sbuf("IDXJ", [128, 16], I32)
        IDXG = E.sbuf("IDXG", [128, 16], F32)
        SELR = E.sbuf("SELR", [64, 16], F32)
        B1ALL = E.sbuf("B1ALL", [128, 66], F32)
        B2ALL = E.sbuf("B2ALL", [128, 96], F32)
        ID128 = E.sbuf("ID128", [128, 128], F32)
        B1S = E.sbuf("B1S", [128, 22], F32)
        B2S = E.sbuf("B2S", [128, 32], F32)
        B2C = E.sbuf("B2C", [128, 16], F32)
        ONE1 = E.sbuf("ONE1", [128, 1], F32)
        PS = [E.psum(f"ps{i}", [128, 512]) for i in range(8)]
        for nm in ["ld", "gd", "ts", "vs", "ss", "tq", "od"]:
            E.sem(nm)
        ld, gd, ts, vs, ss, tq, od = (E.SEM[k] for k in
                                      ["ld", "gd", "ts", "vs", "ss", "tq", "od"])
        inc, C = E.inc, E.C
        block = E.es.enter_context(E.nc.Block())

        @block.sync
        def _(sync):
            inc(sync.dma_start(out=XR[:, :], in_=cm(x_in, 0, ROWS, 0, D)), "ld", 16)
            inc(sync.dma_start(out=GWTS[:, :], in_=gwt[:, :]), "ld", 16)
            inc(sync.dma_start(out=IOW_[:, :], in_=iow[:, :]), "ld", 16)
            inc(sync.dma_start(out=SELR[:, :], in_=selr[:, :]), "ld", 16)
            inc(sync.dma_start(out=ID128[:, :], in_=id128[:, :]), "ld", 16)
            inc(sync.dma_start(out=B1ALL[:, :], in_=cm(b1t_in, 0, NE * 128, 0, 11)), "ld", 16)
            inc(sync.dma_start(out=B2ALL[:, :], in_=cm(b2t_in, 0, NE * 128, 0, 16)), "ld", 16)
            for s2 in range(2):
                for dch in range(16):
                    ins = sync.dma_start(
                        out=BB[:, (s2 * 16 + dch) * 512:(s2 * 16 + dch + 1) * 512],
                        in_=n_in[s2 * 512:(s2 + 1) * 512, dch * 128:(dch + 1) * 128],
                        transpose=True)
                    inc(ins, "tq", 16)
        NSLAB = C["tq"]

        @block.vector
        def _(v):
            v.wait_ge(ld, C["ld"])
            inc(v.memset(ONE1[:, :], 1.0), "vs", 1)

        @block.tensor
        def _(tn):
            tn.wait_ge(tq, NSLAB)

        emit_ffn(E, block, XR, BA, BB, BD, BE, WS, ST_, GS, G128, GWTS, MEANT,
                 IOW_, IDXF, IDXI, IDXJ, IDXG, B1S, B2S, B2C, ONE1, PS,
                 g_all, w1f, w2f, B1ALL, B2ALL, gbd, b2cd, w1e, w2e, arg_s,
                 SELR, ID128)
        FFNV = C["vs"]

        if FFN_STOP == 3 or FFN_STOP == 0:
            SRCB = BA if FFN_STOP == 3 else BB
            @block.vector
            def _(v):
                v.wait_ge(ss, C["ss"])
                v.wait_ge(tq, C["tq"])
                for t in range(NT):
                    ins = v.tensor_copy(XR[:, t * D:(t + 1) * D],
                                        SRCB[:, t * D:(t + 1) * D])
                inc(ins, "vs", 1)

        @block.sync
        def _(sync):
            sync.wait_ge(vs, C["vs"])
            sync.wait_ge(ts, C["ts"])
            sync.wait_ge(ss, C["ss"])
            sync.wait_ge(gd, C["gd"])
            inc(sync.dma_start(out=cm(x_out, 0, ROWS, 0, D), in_=XR[:, :]), "od", 16)
        XOUT = C["od"]

        if FFN_STOP <= 5:
            @block.sync
            def _(sync):
                sync.wait_ge(od, XOUT)
                sync.wait_ge(tq, C["tq"])
            return E.nc

        if final:
            @block.sync
            def _(sync):
                sync.wait_ge(od, XOUT)
            return E.nc

        ln = emit_ln(E, block, XR, BE, BD, ST_, ONE1, PS, n_out,
                     with_gate=(not with_proj))

        if not with_proj:
            @block.vector
            def _(v):
                v.wait_ge(ts, ln["gate_ts"])
                v.tensor_copy(ST_[0:1, 62:63], PS[3][0:1, 0:1])
                v.tensor_copy(ST_[0:1, 61:62], ST_[0:1, 62:63])
                for ch in range(4):
                    ins = v.tensor_scalar(out=BE[0:1, ch * 512:(ch + 1) * 512],
                                          in0=PS[4 + ch][0:1, :], scalar1=ST_[0:1, 62:63],
                                          scalar2=None, op0=ALU.add)
                inc(ins, "vs", 1)
            GEV = C["vs"]

            @block.sync
            def _(sync):
                sync.wait_ge(vs, GEV)
                inc(sync.dma_start(out=g_out[0:1, :], in_=BE[0:1, 0:2048]), "od", 16)

        if with_proj:
            @block.sync
            def _(sync):
                sync.wait_ge(ld, ln["dma_done"][-1])
                for s2 in range(2):
                    for dch in range(16):
                        ins = sync.dma_start(
                            out=BB[:, (s2 * 16 + dch) * 512:(s2 * 16 + dch + 1) * 512],
                            in_=n_out[s2 * 512:(s2 + 1) * 512, dch * 128:(dch + 1) * 128],
                            transpose=True)
                        inc(ins, "tq", 16)
            N3SL = C["tq"]

            @block.gpsimd
            def _(g):
                g.wait_ge(ts, C["ts"])
                inc(g.dma_start(out=WS[:, 0:8192], in_=cm(wkb, 0, D, 0, DKV)), "gd", 16)
            WKB_GD = C["gd"]

            @block.tensor
            def _(tn):
                tn.wait_ge(gd, WKB_GD)
                tn.wait_ge(tq, N3SL)
                tn.wait_ge(vs, C["vs"])
                tn.wait_ge(ss, C["ss"])
                for kv4 in range(4):
                    for s2 in range(2):
                        bank = kv4 * 2 + s2
                        for dch in range(16):
                            ins = tn.matmul(
                                PS[bank][:, :],
                                WS[:, dch * 512 + kv4 * 128:dch * 512 + (kv4 + 1) * 128],
                                BB[:, (s2 * 16 + dch) * 512:(s2 * 16 + dch + 1) * 512],
                                start=(dch == 0), stop=(dch == 15))
                inc(ins, "ts", 1)
            KMM = C["ts"]

            @block.vector
            def _(v):
                v.wait_ge(ts, KMM)
                for kv4 in range(4):
                    for s2 in range(2):
                        ins = v.tensor_copy(
                            BA[:, kv4 * 1024 + s2 * 512:kv4 * 1024 + (s2 + 1) * 512],
                            PS[kv4 * 2 + s2][:, :])
                inc(ins, "vs", 1)
            KEV = C["vs"]

            @block.gpsimd
            def _(g):
                g.wait_ge(ts, KMM)
                inc(g.dma_start(out=WS[:, 0:8192], in_=cm(wvb, 0, D, 0, DKV)), "gd", 16)
            WVB_GD = C["gd"]

            @block.tensor
            def _(tn):
                tn.wait_ge(gd, WVB_GD)
                tn.wait_ge(vs, KEV)
                for st in range(8):
                    s2, st4 = st // 4, st % 4
                    for dch in range(16):
                        ins = tn.matmul(
                            PS[st][:, :],
                            BB[:, (s2 * 16 + dch) * 512 + st4 * 128:(s2 * 16 + dch) * 512 + (st4 + 1) * 128],
                            WS[:, dch * 512:(dch + 1) * 512],
                            start=(dch == 0), stop=(dch == 15))
                inc(ins, "ts", 1)
            VMM = C["ts"]

            @block.vector
            def _(v):
                v.wait_ge(ts, VMM)
                for st in range(8):
                    ins = v.tensor_copy(BA[:, 4096 + st * 512:4096 + (st + 1) * 512],
                                        PS[st][:, :])
                inc(ins, "vs", 1)
            VEV = C["vs"]

            @block.sync
            def _(sync):
                sync.wait_ge(vs, VEV)
                inc(sync.dma_start(out=cm(kT_own, 0, DKV, 0, ROWS), in_=BA[:, 0:4096]),
                    "od", 16)
                inc(sync.dma_start(out=cm(v_own, 0, ROWS, 0, DKV), in_=BA[:, 4096:8192]),
                    "od", 16)

        @block.sync
        def _(sync):
            sync.wait_ge(ld, ln["dma_done"][-1])
            sync.wait_ge(od, C["od"])
    return E.nc


def build_s4():
    E = Env("s4")
    return ffn_stage_common(E, with_proj=True, final=False)


def build_s6():
    E = Env("s6")
    return ffn_stage_common(E, with_proj=False, final=False)


def build_s6_final():
    E = Env("s6f")
    return ffn_stage_common(E, with_proj=False, final=True)


def build_s5():
    E = Env("s5")
    x_in = E.dram("x_in", [ROWS, D], F32, "ExternalInput")
    n_in = E.dram("n_in", [ROWS, D], BF16, "ExternalInput")
    kT_glob = E.dram("kT_glob", [DKV, S], BF16, "ExternalInput")
    v_glob = E.dram("v_glob", [S, DKV], BF16, "ExternalInput")
    wqb = E.dram("wqb", [D, D], F32, "ExternalInput")
    wob = E.dram("wob", [D, D], F32, "ExternalInput")
    mbc0 = E.dram("mbc0", [128, 2432], BF16, "ExternalInput")
    mbc1 = E.dram("mbc1", [128, 2432], BF16, "ExternalInput")
    iow = E.dram("iow", [128, 16], F32, "ExternalInput")
    x_out = E.dram("x_out", [ROWS, D], F32, "ExternalOutput")
    n_out = E.dram("n_out", [ROWS, D], BF16, "ExternalOutput")
    g_out = E.dram("g_out", [1, D], F32, "ExternalOutput")
    dnmd = E.dram("dnmd", [2, 512], F32, "Internal")

    NCH = {0: 16, 1: 32}

    with E.es:
        XR = E.sbuf("XR", [128, NT * D], F32)
        BA = E.sbuf("BA", [128, 16384], BF16)
        BB = E.sbuf("BB", [128, 16384], BF16)
        BD = E.sbuf("BD", [128, 8192], BF16)
        BE = E.sbuf("BE", [128, 2048], F32)
        WS = E.sbuf("WS", [128, 8192], BF16)
        MB0 = E.sbuf("MB0", [128, 2432], BF16)
        MB1 = E.sbuf("MB1", [128, 2432], BF16)
        ST_ = E.sbuf("ST_", [128, 64], F32)
        IOW_ = E.sbuf("IOW_", [128, 16], F32)
        ONE1 = E.sbuf("ONE1", [128, 1], F32)
        ONEB = E.sbuf("ONEB", [128, 1], BF16)
        EXB = E.sbuf("EXB", [128, 1], BF16)
        SUF = E.sbuf("SUF", [128, 4], F32)
        DRW = E.sbuf("DRW", [1, 2048], F32)
        DIV = E.sbuf("DIV", [128, 1024], F32)
        PS = [E.psum(f"ps{i}", [128, 512]) for i in range(8)]
        for nm in ["ld", "gd", "ts", "vs", "ss", "tq", "od"]:
            E.sem(nm)
        ld, gd, ts, vs, ss, tq, od = (E.SEM[k] for k in
                                      ["ld", "gd", "ts", "vs", "ss", "tq", "od"])
        inc, C = E.inc, E.C
        block = E.es.enter_context(E.nc.Block())

        @block.sync
        def _(sync):
            inc(sync.dma_start(out=XR[:, :], in_=cm(x_in, 0, ROWS, 0, D)), "ld", 16)
            inc(sync.dma_start(out=MB0[:, :], in_=mbc0[:, :]), "ld", 16)
            inc(sync.dma_start(out=MB1[:, :], in_=mbc1[:, :]), "ld", 16)
            inc(sync.dma_start(out=IOW_[:, :], in_=iow[:, :]), "ld", 16)
            for s2 in range(2):
                for dch in range(16):
                    ins = sync.dma_start(
                        out=BB[:, (s2 * 16 + dch) * 512:(s2 * 16 + dch + 1) * 512],
                        in_=n_in[s2 * 512:(s2 + 1) * 512, dch * 128:(dch + 1) * 128],
                        transpose=True)
                    inc(ins, "tq", 16)
        NSLAB = C["tq"]
        LD0 = C["ld"]

        @block.vector
        def _(v):
            v.wait_ge(ld, LD0)
            v.memset(ONE1[:, :], 1.0)
            v.memset(ONEB[:, :], 1.0)
            inc(v.memset(EXB[:, :], EXPM50), "vs", 1)
        VS0 = C["vs"]

        q_wl, q_mm, q_ev = [], [], []
        for q4 in range(4):
            @block.gpsimd
            def _(g, q4=q4):
                if q4 >= 1:
                    g.wait_ge(ts, q_mm[-1])
                ins = g.dma_start(out=WS[:, 0:8192],
                                  in_=cm(wqb, 0, D, q4 * 512, (q4 + 1) * 512))
                q_wl.append(inc(ins, "gd", 16))

            @block.tensor
            def _(tn, q4=q4):
                tn.wait_ge(gd, q_wl[q4])
                tn.wait_ge(tq, NSLAB)
                if q4 >= 1:
                    tn.wait_ge(ss, q_ev[-1])
                for ht4 in range(4):
                    for s2 in range(2):
                        bank = ht4 * 2 + s2
                        for dch in range(16):
                            ins = tn.matmul(
                                PS[bank][:, :],
                                WS[:, dch * 512 + ht4 * 128:dch * 512 + (ht4 + 1) * 128],
                                BB[:, (s2 * 16 + dch) * 512:(s2 * 16 + dch + 1) * 512],
                                start=(dch == 0), stop=(dch == 15))
                q_mm.append(inc(ins, "ts", 1))

            @block.scalar
            def _(s, q4=q4):
                s.wait_ge(ts, q_mm[q4])
                for ht4 in range(4):
                    for s2 in range(2):
                        h = q4 * 4 + ht4
                        ins = s.activation(
                            out=BA[:, h * 1024 + s2 * 512:h * 1024 + (s2 + 1) * 512],
                            in_=PS[ht4 * 2 + s2][:, :], func=AF.Identity, scale=RT128)
                q_ev.append(inc(ins, "ss", 1))

        slab_ld = []
        ctx_ev = []
        LASTCLIP = {0: None, 1: None, 2: None}
        LASTEXP = {0: None, 1: None, 2: None}
        LASTCM = {0: None, 1: None, 2: None}
        for kvh in range(NKV):
            @block.sync
            def _(sync, kvh=kvh):
                if kvh == 0:
                    sync.wait_ge(ts, q_mm[-1])
                else:
                    sync.wait_ge(ts, C["ts"])
                ins = sync.dma_start(out=BD[:, 0:4096],
                                     in_=kT_glob[kvh * 128:(kvh + 1) * 128, :])
                inc(ins, "ld", 16)
                ins = sync.dma_start(out=BD[:, 4096:8192],
                                     in_=cm(v_glob, 0, S, kvh * 128, (kvh + 1) * 128))
                slab_ld.append(inc(ins, "ld", 16))

            @block.tensor
            def _(tn, kvh=kvh):
                tn.wait_ge(ld, slab_ld[kvh])
                if kvh == 0:
                    tn.wait_ge(ss, q_ev[-1])
                else:
                    tn.wait_ge(vs, C["vs"])
                for c_ in range(16, 32):
                    ins = tn.matmul(PS[7][:, 0:1],
                                    BD[:, 4096 + c_ * 128:4096 + (c_ + 1) * 128],
                                    EXB[:, :],
                                    start=(c_ == 16), stop=(c_ == 31))
                inc(ins, "ts", 1)
            SUFT = C["ts"]

            @block.vector
            def _(v, kvh=kvh):
                v.wait_ge(ts, SUFT)
                ins = v.tensor_copy(SUF[:, kvh:kvh + 1], PS[7][:, 0:1])
                inc(ins, "vs", 1)
            SUFV = C["vs"]

            for h4 in range(4):
                h = kvh * 4 + h4
                for qt in range(2):
                    nch = NCH[qt]
                    par = (h * 2 + qt) % 2
                    ctx_b, den_b = 3 + 2 * par, 4 + 2 * par
                    clip_done = [None] * nch
                    exp_done = [None] * nch
                    mm_done = [None] * nch
                    cm_done = [None] * nch
                    for c_ in range(nch):
                        sb = c_ % 3

                        @block.tensor
                        def _(tn, kvh=kvh, h=h, qt=qt, c_=c_, sb=sb, nch=nch,
                              ctx_b=ctx_b, den_b=den_b):
                            if LASTCLIP[sb] is not None:
                                tn.wait_ge(vs, LASTCLIP[sb])
                            if h4 == 0 and qt == 0 and c_ == 0:
                                tn.wait_ge(vs, SUFV)
                            ins = tn.matmul(PS[sb][:, :],
                                            BD[:, c_ * 128:(c_ + 1) * 128],
                                            BA[:, h * 1024 + qt * 512:h * 1024 + (qt + 1) * 512],
                                            start=True, stop=True)
                            mm_done[c_] = inc(ins, "ts", 1)

                        @block.vector
                        def _(v, kvh=kvh, h=h, qt=qt, c_=c_, sb=sb, nch=nch):
                            v.wait_ge(ts, mm_done[c_])
                            if LASTEXP[sb] is not None:
                                v.wait_ge(ss, LASTEXP[sb])
                            masked = (qt == 0) or (c_ >= 16)
                            if masked:
                                mb = MB0 if qt == 0 else MB1
                                off = 1920 - 128 * (c_ if qt == 0 else c_ - 16)
                                v.scalar_tensor_tensor(
                                    out=BE[:, sb * 512:(sb + 1) * 512],
                                    in0=PS[sb][:, :], scalar=50.0,
                                    in1=mb[:, off:off + 512],
                                    op0=ALU.min, op1=ALU.add)
                                ins = v.tensor_scalar(
                                    out=BE[:, sb * 512:(sb + 1) * 512],
                                    in0=BE[:, sb * 512:(sb + 1) * 512],
                                    scalar1=-50.0, scalar2=50.0,
                                    op0=ALU.max, op1=ALU.min)
                            else:
                                ins = v.tensor_scalar(
                                    out=BE[:, sb * 512:(sb + 1) * 512],
                                    in0=PS[sb][:, :],
                                    scalar1=-50.0, scalar2=50.0,
                                    op0=ALU.max, op1=ALU.min)
                            clip_done[c_] = inc(ins, "vs", 1)
                            LASTCLIP[sb] = clip_done[c_]

                        @block.scalar
                        def _(s, c_=c_, sb=sb):
                            s.wait_ge(vs, clip_done[c_])
                            if LASTCM[sb] is not None:
                                s.wait_ge(ts, LASTCM[sb])
                            ins = s.activation(out=WS[:, 6144 + sb * 512:6144 + (sb + 1) * 512],
                                               in_=BE[:, sb * 512:(sb + 1) * 512],
                                               func=AF.Exp)
                            exp_done[c_] = inc(ins, "ss", 1)
                            LASTEXP[sb] = exp_done[c_]

                        @block.tensor
                        def _(tn, c_=c_, sb=sb, ctx_b=ctx_b, den_b=den_b, nch=nch,
                              h=h, qt=qt):
                            tn.wait_ge(ss, exp_done[c_])
                            if c_ == 0 and len(ctx_ev) >= 2:
                                tn.wait_ge(vs, ctx_ev[-2])
                            ins = tn.matmul(PS[ctx_b][:, :],
                                            BD[:, 4096 + c_ * 128:4096 + (c_ + 1) * 128],
                                            WS[:, 6144 + sb * 512:6144 + (sb + 1) * 512],
                                            start=(c_ == 0), stop=(c_ == nch - 1))
                            ins2 = tn.matmul(PS[den_b][0:1, :],
                                             ONEB[:, :],
                                             WS[:, 6144 + sb * 512:6144 + (sb + 1) * 512],
                                             start=(c_ == 0), stop=(c_ == nch - 1))
                            cm_done[c_] = inc(ins2, "ts", 1)
                            LASTCM[sb] = cm_done[c_]

                    @block.vector
                    def _(v, qt=qt, den_b=den_b, nch=nch):
                        v.wait_ge(ts, cm_done[nch - 1])
                        corr = EXPM50 * 2048.0 if qt == 0 else 0.0
                        v.tensor_scalar_add(DRW[0:1, qt * 512:(qt + 1) * 512],
                                            PS[den_b][0:1, :], corr)
                        v.tensor_copy(ST_[:, 0:2], IOW_[:, 0:2])
                        ins = v.reciprocal(DRW[0:1, (2 + qt) * 512:(3 + qt) * 512],
                                           DRW[0:1, qt * 512:(qt + 1) * 512])
                        inc(ins, "vs", 1)
                    DRV = C["vs"]

                    @block.sync
                    def _(sync, qt=qt):
                        sync.wait_ge(vs, DRV)
                        ins = sync.dma_start(out=dnmd[qt:qt + 1, :],
                                             in_=DRW[0:1, (2 + qt) * 512:(3 + qt) * 512])
                        inc(ins, "ld", 16)
                        sync.wait_ge(ld, C["ld"])
                        ins = sync.dma_start(out=DIV[:, qt * 512:(qt + 1) * 512],
                                             in_=pbc(dnmd[qt:qt + 1, :]))
                        inc(ins, "ld", 16)
                    DIVL = C["ld"]

                    @block.vector
                    def _(v, kvh=kvh, h=h, qt=qt, ctx_b=ctx_b):
                        v.wait_ge(ld, DIVL)
                        if qt == 0:
                            v.scalar_tensor_tensor(
                                out=BE[:, 1536:2048], in0=PS[ctx_b][:, :],
                                scalar=SUF[:, kvh:kvh + 1], in1=DIV[:, 0:512],
                                op0=ALU.add, op1=ALU.mult)
                        else:
                            v.tensor_tensor(BE[:, 1536:2048], PS[ctx_b][:, :],
                                            DIV[:, 512:1024], op=ALU.mult)
                        v.tensor_copy(ST_[:, 0:2], IOW_[:, 0:2])
                        ins = v.tensor_copy(
                            BB[:, h * 1024 + qt * 512:h * 1024 + (qt + 1) * 512],
                            BE[:, 1536:2048])
                        ctx_ev.append(inc(ins, "vs", 1))

        y3_wl, y3_mm, y3_ev = [], [], []
        for q4 in range(4):
            @block.gpsimd
            def _(g, q4=q4):
                if q4 == 0:
                    g.wait_ge(vs, ctx_ev[-1])
                else:
                    g.wait_ge(ts, y3_mm[-1])
                ins = g.dma_start(out=WS[:, 0:8192],
                                  in_=cm(wob, 0, D, q4 * 512, (q4 + 1) * 512))
                y3_wl.append(inc(ins, "gd", 16))

            @block.tensor
            def _(tn, q4=q4):
                tn.wait_ge(gd, y3_wl[q4])
                if q4 == 0:
                    tn.wait_ge(vs, ctx_ev[-1])
                else:
                    tn.wait_ge(vs, y3_ev[-1])
                for st in range(8):
                    for hh in range(16):
                        ins = tn.matmul(PS[st][:, :],
                                        BB[:, hh * 1024 + st * 128:hh * 1024 + (st + 1) * 128],
                                        WS[:, hh * 512:(hh + 1) * 512],
                                        start=(hh == 0), stop=(hh == 15))
                y3_mm.append(inc(ins, "ts", 1))

            @block.vector
            def _(v, q4=q4):
                v.wait_ge(ts, y3_mm[q4])
                for st in range(8):
                    ins = v.tensor_tensor(XR[:, st * D + q4 * 512:st * D + (q4 + 1) * 512],
                                          XR[:, st * D + q4 * 512:st * D + (q4 + 1) * 512],
                                          PS[st][:, :], op=ALU.add)
                y3_ev.append(inc(ins, "vs", 1))

        @block.sync
        def _(sync):
            sync.wait_ge(vs, y3_ev[-1])
            inc(sync.dma_start(out=cm(x_out, 0, ROWS, 0, D), in_=XR[:, :]), "od", 16)

        ln = emit_ln(E, block, XR, BE, BD, ST_, ONE1, PS, n_out, with_gate=True)

        @block.vector
        def _(v):
            v.wait_ge(ts, ln["gate_ts"])
            v.tensor_copy(ST_[0:1, 62:63], PS[3][0:1, 0:1])
            v.tensor_copy(ST_[0:1, 61:62], ST_[0:1, 62:63])
            for ch in range(4):
                ins = v.tensor_scalar(out=BE[0:1, ch * 512:(ch + 1) * 512],
                                      in0=PS[4 + ch][0:1, :], scalar1=ST_[0:1, 62:63],
                                      scalar2=None, op0=ALU.add)
            inc(ins, "vs", 1)
        GEV = C["vs"]

        @block.sync
        def _(sync):
            sync.wait_ge(vs, GEV)
            inc(sync.dma_start(out=g_out[0:1, :], in_=BE[0:1, 0:2048]), "od", 16)
            sync.wait_ge(ld, ln["dma_done"][-1])
            sync.wait_ge(od, C["od"])
    return E.nc


def own_strips(r):
    return (r, 7 - r)


def strip_loc(t):
    return (t, 0) if t < 4 else (7 - t, 512)


def host_prep(inputs):
    c = np.ascontiguousarray
    P = {}
    P["x"] = np.asarray(inputs["L0_attn_a_x"])
    for nm, key in [("wqa", "L0_attn_a_Wq"), ("wka", "L0_attn_a_Wk"),
                    ("wva", "L0_attn_a_Wv"), ("woa", "L0_attn_a_Wo"),
                    ("wqb", "L0_attn_b_Wq"), ("wkb", "L0_attn_b_Wk"),
                    ("wvb", "L0_attn_b_Wv"), ("wob", "L0_attn_b_Wo")]:
        P[nm] = np.asarray(inputs[key])
    gw = np.asarray(inputs["L0_ffn_gate_W"])
    w1, b1 = np.asarray(inputs["L0_ffn_W1"]), np.asarray(inputs["L0_ffn_b1"])
    w2, b2 = np.asarray(inputs["L0_ffn_W2"]), np.asarray(inputs["L0_ffn_b2"])
    P["w1f"] = c(w1.reshape(NE * D, DFF))
    P["w2f"] = c(w2.reshape(NE * DFF, D))
    P["b1t"] = c(b1.reshape(NE, 11, 128).transpose(0, 2, 1).reshape(NE * 128, 11))
    P["b2t"] = c(b2.reshape(NE, 16, 128).transpose(0, 2, 1).reshape(NE * 128, 16))
    P["gwt"] = c(gw.reshape(NE, 16, 128).transpose(2, 1, 0).reshape(128, 16 * NE))
    P["iow"] = (np.arange(128).reshape(128, 1)
                + 128.0 * np.arange(16).reshape(1, 16)).astype(np.float32)
    ii = np.arange(128).reshape(128, 1)
    ww = np.arange(2432).reshape(1, 2432)
    selr = np.zeros((64, 16), dtype=np.float32)
    for r_ in range(4):
        for c_ in range(16):
            selr[r_ * 16 + c_, c_] = 1.0
    P["selr"] = selr
    P["id128"] = np.eye(128, dtype=np.float32)
    P["mbc0"] = [np.where(ii + 1920 - ww - 512 * r > 0, -1e6, 0.0
                          ).astype(ml_dtypes.bfloat16) for r in range(4)]
    P["mbc1"] = [np.where(ii + 384 - ww + 512 * r > 0, -1e6, 0.0
                          ).astype(ml_dtypes.bfloat16) for r in range(4)]
    P["x_own"] = []
    for core in range(8):
        g, r = core // 4, core % 4
        sA, sB = own_strips(r)
        P["x_own"].append(c(np.concatenate(
            [P["x"][g, STRIP * sA:STRIP * (sA + 1)],
             P["x"][g, STRIP * sB:STRIP * (sB + 1)]], axis=0)))
    return P


def run_staged(inputs, run_spmd, upto=7, cache=None):
    c = np.ascontiguousarray
    P = host_prep(inputs)

    def group_cat(outs, key, axis=0):
        return [c(np.concatenate([outs[4 * g + r][key] for r in range(4)], axis=axis))
                for g in range(2)]

    import os, pickle

    def cached(stage, fn):
        if cache is None:
            return fn()
        fp = f"/tmp/_stage_{stage}.pkl"
        if os.path.exists(fp) and stage in cache:
            return pickle.load(open(fp, "rb"))
        r = fn()
        pickle.dump(r, open(fp, "wb"))
        cache.add(stage)
        return r

    r0 = cached("s0", lambda: run_spmd(build_s0(),
                [{"x_own": P["x_own"][cc]} for cc in range(8)]))
    n_own = [r0[cc]["o_n"] for cc in range(8)]
    if upto == 0:
        return None, {"n_own": n_own}
    n_full = group_cat(r0, "o_n")

    im = []
    for cc in range(8):
        g, r = cc // 4, cc % 4
        im.append({"n_full": n_full[g], "wqa": c(P["wqa"][:, 512 * r:512 * (r + 1)]),
                   "wka": P["wka"]})
    r1 = cached("s1", lambda: run_spmd(build_s1(), im))
    if upto == 1:
        return None, {"a_own": [r1[cc]["a_own"] for cc in range(8)], "n_own": n_own,
                      "q": [r1[cc].get("q_dbg") for cc in range(8)],
                      "pt": [r1[cc].get("pt_dbg") for cc in range(8)]}
    a_full = group_cat(r1, "a_own")

    im = []
    for cc in range(8):
        g, r = cc // 4, cc % 4
        im.append({"a_full": a_full[g], "woa": c(P["woa"][:, 512 * r:512 * (r + 1)])})
    r2 = cached("s2", lambda: run_spmd(build_s2(), im))
    if upto == 2:
        return None, {"m_own": [r2[cc]["m_own"] for cc in range(8)]}
    m_full = group_cat(r2, "m_own")

    im = []
    for cc in range(8):
        g = cc // 4
        im.append({"x_own": P["x_own"][cc], "n_own": n_own[cc], "m_full": m_full[g],
                   "wva": P["wva"]})
    r3 = cached("s3", lambda: run_spmd(build_s3(), im))
    x1_own = [r3[cc]["x1_own"] for cc in range(8)]
    n2_own = [r3[cc]["n2_own"] for cc in range(8)]
    g1_all = [c(np.concatenate([r3[4 * g + r]["g1p"] for r in range(4)], axis=0))
              for g in range(2)]
    if upto == 3:
        return None, {"x1_own": x1_own, "n2_own": n2_own, "g1_all": g1_all}

    im = []
    for cc in range(8):
        g = cc // 4
        im.append({"x_in": x1_own[cc], "n_in": n2_own[cc],
                   "g_all": c(g1_all[g].reshape(4, 16, 128).reshape(64, 128)),
                   "selr": P["selr"], "id128": P["id128"],
                   "w1f": P["w1f"], "w2f": P["w2f"], "b1t": P["b1t"], "b2t": P["b2t"],
                   "gwt": P["gwt"], "iow": P["iow"], "wkb": P["wkb"], "wvb": P["wvb"]})
    r4 = cached("s4", lambda: run_spmd(build_s4(), im))
    x2_own = [r4[cc]["x_out"] for cc in range(8)]
    n3_own = [r4[cc]["n_out"] for cc in range(8)]

    if upto == 4:
        return None, {"x2_own": x2_own, "n3_own": n3_own,
                      "kT_own": [r4[cc]["kT_own"] for cc in range(8)],
                      "v_own": [r4[cc]["v_own"] for cc in range(8)]}
    kT_glob, v_glob = [], []
    for g in range(2):
        kt = np.zeros((DKV, S), dtype=ml_dtypes.bfloat16)
        vv = np.zeros((S, DKV), dtype=ml_dtypes.bfloat16)
        for t in range(8):
            r, off = strip_loc(t)
            kt[:, 512 * t:512 * (t + 1)] = r4[4 * g + r]["kT_own"][:, off:off + 512]
            vv[512 * t:512 * (t + 1), :] = r4[4 * g + r]["v_own"][off:off + 512, :]
        kT_glob.append(c(kt))
        v_glob.append(c(vv))

    im = []
    for cc in range(8):
        g, r = cc // 4, cc % 4
        im.append({"x_in": x2_own[cc], "n_in": n3_own[cc], "kT_glob": kT_glob[g],
                   "v_glob": v_glob[g], "wqb": P["wqb"], "wob": P["wob"],
                   "mbc0": P["mbc0"][r], "mbc1": P["mbc1"][r], "iow": P["iow"]})
    r5 = cached("s5", lambda: run_spmd(build_s5(), im))
    x3_own = [r5[cc]["x_out"] for cc in range(8)]
    n4_own = [r5[cc]["n_out"] for cc in range(8)]
    g2_all = [c(np.concatenate([r5[4 * g + r]["g_out"] for r in range(4)], axis=0))
              for g in range(2)]
    if upto == 5:
        return None, {"x3_own": x3_own, "n4_own": n4_own, "g2_all": g2_all}

    im = []
    for cc in range(8):
        g = cc // 4
        im.append({"x_in": x3_own[cc], "n_in": n4_own[cc],
                   "g_all": c(g2_all[g].reshape(4, 16, 128).reshape(64, 128)),
                   "selr": P["selr"], "id128": P["id128"],
                   "w1f": P["w1f"], "w2f": P["w2f"], "b1t": P["b1t"], "b2t": P["b2t"],
                   "gwt": P["gwt"], "iow": P["iow"]})
    r6 = run_spmd(build_s6_final(), im)

    y = np.zeros((B, S, D), dtype=np.float32)
    for core in range(8):
        g, r = core // 4, core % 4
        sA, sB = own_strips(r)
        yo = np.asarray(r6[core]["x_out"])
        y[g, STRIP * sA:STRIP * (sA + 1)] = yo[0:512]
        y[g, STRIP * sB:STRIP * (sB + 1)] = yo[512:1024]
    return y, None


def kernel(**inputs):
    from concourse.bass_utils import run_bass_kernel_spmd

    def run_spmd(nc, in_maps):
        res = run_bass_kernel_spmd(nc, in_maps, core_ids=list(range(8)))
        return [{k: np.asarray(v) for k, v in r.items()} for r in res.results]

    y, _ = run_staged(inputs, run_spmd, upto=7)
    return y
